# version 18 (final)
# speedup vs baseline: 1.3102x; 1.0107x over previous
import os
import numpy as np
import ml_dtypes
import concourse.bass as bass
import concourse.mybir as mybir
from concourse.bass_utils import run_bass_kernel_spmd
from contextlib import ExitStack

F32 = mybir.dt.float32
BF16 = mybir.dt.bfloat16
AF = mybir.ActivationFunctionType
ALU = mybir.AluOpType
AX = mybir.AxisListType

P = 128
D = 1024
L = 4096
NG = 8
HC = 8
E = 8
CH = 512
NCH = L // CH
TPC = CH // P
W = 32
ISUB = CH // W
N = 16
EPS = 1e-5
BIG = 1e22
NSQ = 5

N_CORES = 8


def build_nc():
    nc = bass.Bass("TRN2", target_bir_lowering=False, debug=False,
                   detect_race_conditions=False)

    x_d = nc.dram_tensor("x", [L, D], F32, kind="ExternalInput").ap()
    A_d = nc.dram_tensor("A", [D, N, N], F32, kind="ExternalInput").ap()
    Bp_d = nc.dram_tensor("Bp", [D, N], F32, kind="ExternalInput").ap()
    Cp_d = nc.dram_tensor("Cp", [D, N], F32, kind="ExternalInput").ap()
    Dp_d = nc.dram_tensor("Dp", [D], F32, kind="ExternalInput").ap()
    Wg_d = nc.dram_tensor("Wg", [D, D], F32, kind="ExternalInput").ap()
    bg_d = nc.dram_tensor("bg", [D], F32, kind="ExternalInput").ap()
    gam_d = nc.dram_tensor("gamma", [D], F32, kind="ExternalInput").ap()
    bet_d = nc.dram_tensor("beta", [D], F32, kind="ExternalInput").ap()
    Wr_d = nc.dram_tensor("Wr", [E, D], F32, kind="ExternalInput").ap()
    EW_d = nc.dram_tensor("EW", [E, D, D], F32, kind="ExternalInput").ap()
    out_d = nc.dram_tensor("out", [L, D], F32, kind="ExternalOutput").ap()
    ews_d = nc.dram_tensor("ewscr", [E, D, D], BF16, kind="Internal").ap()
    DBG = bool(int(os.environ.get("KDBG", "0")))
    if DBG:
        dY_d = nc.dram_tensor("dY", [NCH, P, NG * CH], F32, kind="ExternalOutput").ap()
        dFF_d = nc.dram_tensor("dFF", [NCH, P, TPC * D], F32, kind="ExternalOutput").ap()
        dXG_d = nc.dram_tensor("dXG", [NCH, P, NG * CH], BF16, kind="ExternalOutput").ap()
        dLN_d = nc.dram_tensor("dLN", [NCH, TPC, P, D], F32, kind="ExternalOutput").ap()
        dRm_d = nc.dram_tensor("dRm", [NCH, P, TPC * E], F32, kind="ExternalOutput").ap()
        dRT_d = nc.dram_tensor("dRT", [NCH, P, TPC * E], F32, kind="ExternalOutput").ap()
        dRE_d = nc.dram_tensor("dRE", [NCH, P, TPC * E], F32, kind="ExternalOutput").ap()
        dSel_d = nc.dram_tensor("dSel", [NCH, P, TPC * E], F32, kind="ExternalOutput").ap()
        dXNF_d = nc.dram_tensor("dXNF", [NCH, P, HC * CH], BF16, kind="ExternalOutput").ap()
        dRFM_d = nc.dram_tensor("dRFM", [NCH, E, CH], F32, kind="ExternalOutput").ap()
        dXN_d = nc.dram_tensor("dXN", [NCH, P, TPC * D], BF16, kind="ExternalOutput").ap()
        dGE_d = nc.dram_tensor("dGE", [NCH, P, CH], BF16, kind="ExternalOutput").ap()
        dLS_d = nc.dram_tensor("dLS", [NCH, P, TPC * 8], F32, kind="ExternalOutput").ap()
        dK_d = nc.dram_tensor("dK", [P, NG * W], F32, kind="ExternalOutput").ap()
        dU_d = nc.dram_tensor("dU", [P, NG * N * W], BF16, kind="ExternalOutput").ap()
        dC_d = nc.dram_tensor("dC", [P, NG * W * N], BF16, kind="ExternalOutput").ap()
        dS_d = nc.dram_tensor("dS", [NCH, P, NG * ISUB * N], BF16, kind="ExternalOutput").ap()
        dRs_d = nc.dram_tensor("dRs", [NCH, P, TPC * 8], F32, kind="ExternalOutput").ap()

    ident_bf_d = nc.inline_tensor(np.eye(P, dtype=ml_dtypes.bfloat16), name="idbf")
    ident_f_d = nc.inline_tensor(np.eye(P, dtype=np.float32), name="idf_d")

    with ExitStack() as st:
        def sbuf(name, shape, dtype=F32):
            return st.enter_context(nc.sbuf_tensor(name, shape, dtype))

        def psum(name, shape, dtype=F32):
            return st.enter_context(nc.psum_tensor(name, shape, dtype))

        def sem(name):
            return st.enter_context(nc.semaphore(name))

        WgT = sbuf("WgT", [P, HC * D], BF16)
        WrT = sbuf("WrT", [P, HC * E], BF16)
        gbc = sbuf("gbc", [P, D], BF16)
        bbc = sbuf("bbc", [P, D], BF16)
        nbg = sbuf("nbg", [P, NG])
        Dpf = sbuf("Dpf", [P, NG])
        Bpf = sbuf("Bpf", [P, NG * N])
        Cpf = sbuf("Cpf", [P, NG * N])
        Afm = sbuf("Afm", [P, NG * N * N])
        idb = sbuf("idb", [P, P], BF16)
        idf = sbuf("idf", [P, P])
        Wbuf = [sbuf(f"Wbuf{i}", [P, HC * 512], BF16) for i in range(4)]
        stgbb = sbuf("stgbb", [P, 2 * D], BF16)
        stgb = [stgbb[:, 0:D], stgbb[:, D:2 * D]]
        U2 = sbuf("U2", [P, NG * N * W], BF16)
        C2 = sbuf("C2", [P, NG * W * N], BF16)
        ktab = sbuf("ktab", [P, NG * W])
        M2T = sbuf("M2T", [P, NG * N * N], BF16)
        cu0 = sbuf("cu0", [P, NG * N])
        cu1 = sbuf("cu1", [P, NG * N])
        SCR = sbuf("SCR", [P, 4096])
        stg = [SCR[:, 0:D], SCR[:, D:2 * D]]
        wrstg = SCR[0:E, 2 * D:3 * D]
        wrbf = SCR[:, 3 * D:3 * D + 512].bitcast(BF16)
        S = sbuf("S", [P, NG * ISUB * N], BF16)
        Scar = sbuf("Scar", [P, NG * N], BF16)
        G = sbuf("G", [P, NG * ISUB * N], BF16)
        XT = [sbuf(f"XT{i}", [P, TPC * D]) for i in range(2)]
        XN = sbuf("XN", [P, TPC * D], BF16)
        XNF = sbuf("XNF", [P, HC * CH], BF16)
        XGF = sbuf("XGF", [P, NG * CH], BF16)
        GE1 = sbuf("GE1", [P, CH], BF16)
        Yst = sbuf("Yst", [P, CH])
        Y = sbuf("Y", [P, NG * CH])
        FF = sbuf("FF", [P, TPC * D], BF16)
        tmpbf = [sbuf(f"tmpbf{i}", [P, 512], BF16) for i in range(2)]
        tmpbf2 = [stgbb[:, 1024:1536], stgbb[:, 1536:2048]]
        cub = sbuf("cub", [P, NG * N], BF16)
        LNT = Afm[:, 0:D]
        sqd = sbuf("sqd", [P, D], BF16)
        RFM = sbuf("RFM", [E, CH])
        RT = sbuf("RT", [P, TPC * E])
        RE = sbuf("RE", [P, TPC * E])
        RSel = sbuf("RSel", [P, TPC * E])
        Rm = sbuf("Rm", [P, TPC * E])
        Rsc = sbuf("Rsc", [P, TPC * 8])
        LNS = sbuf("LNS", [P, TPC * 8])
        LNS2 = sbuf("LNS2", [P, TPC * 8])
        epsc = sbuf("epsc", [P, 1])
        padt = sbuf("padt", [P, 1024])
        gpt = stgbb[:, 0:1024]

        pb_tb = psum("pb_tb", [P, 8 * P], BF16)
        pb_t = psum("pb_t", [P, 512])
        pb_g = [psum(f"pb_g{i}", [P, 512]) for i in range(2)]
        pb_e = [psum(f"pb_e{i}", [P, 512]) for i in range(4)]

        s_x = [sem("s_x0"), sem("s_x1")]
        s_w = [sem(f"s_wq{i}") for i in range(4)]
        s_prm = sem("s_prm")
        s_stg = [sem("s_stg0"), sem("s_stg1")]
        s_scst = [sem("s_scst0"), sem("s_scst1")]
        s_cst = sem("s_cst")
        s_ini = sem("s_ini"); s_gbm = sem("s_gbm"); s_wgt = sem("s_wgt")
        s_sq = sem("s_sq"); s_lnv = sem("s_lnv"); s_lnr = sem("s_lnr")
        s_lns = sem("s_lns"); s_ln = sem("s_ln")
        s_tpp = sem("s_tpp"); s_tpe = sem("s_tpe")
        s_gate = sem("s_gate"); s_gexp = sem("s_gexp"); s_xg = sem("s_xg")
        s_rt = sem("s_rt"); s_rte = sem("s_rte"); s_rtt = sem("s_rtt")
        s_rx1 = sem("s_rx1"); s_rx2 = sem("s_rx2")
        s_rm = sem("s_rm"); s_ffi = sem("s_ffi")
        s_mm = sem("s_mm"); s_cmb = sem("s_cmb"); s_cev = sem("s_cev")
        s_y2 = sem("s_y2"); s_ytp = sem("s_ytp"); s_yte = sem("s_yte")
        s_gc = sem("s_gc"); s_gcd = sem("s_gcd")
        s2_sq = sem("s2_sq"); s2_v = sem("s2_v"); s2_r = sem("s2_r")
        s2_s = sem("s2_s"); s2_ap = sem("s2_ap")
        s_fin = sem("s_fin"); s_out = sem("s_out"); s_dbg = sem("s_dbg")

        block = st.enter_context(nc.Block())

        N_PRM = 8 * 5 + 2 + 1 + 2
        NJ = 8 + E * HC

        def v2(_t, _spec, **kw):
            return _t[:, :].rearrange(_spec, **kw)

        @block.sync
        def _(sync):
            sync.dma_start(
                out=v2(XT[0], "p (t f) -> p t f", t=TPC),
                in_=x_d[0:CH, :].rearrange("(t p) f -> p t f", p=P),
            ).then_inc(s_x[0], 16)
            for g in range(NG):
                sync.dma_start(
                    out=Afm[:, g * N * N:(g + 1) * N * N],
                    in_=A_d.rearrange("(g p) n m -> g p (n m)", p=P)[g],
                ).then_inc(s_prm, 16)
            for g in range(NG):
                sync.dma_start(
                    out=Bpf[:, g * N:(g + 1) * N],
                    in_=Bp_d.rearrange("(g p) n -> g p n", p=P)[g],
                ).then_inc(s_prm, 16)
            for g in range(NG):
                sync.dma_start(
                    out=Cpf[:, g * N:(g + 1) * N],
                    in_=Cp_d.rearrange("(g p) n -> g p n", p=P)[g],
                ).then_inc(s_prm, 16)
            for g in range(NG):
                sync.dma_start(
                    out=Dpf[:, g:g + 1],
                    in_=Dp_d.rearrange("(g p) -> g p", p=P)[g].unsqueeze(1),
                ).then_inc(s_prm, 16)
            for g in range(NG):
                sync.dma_start(
                    out=nbg[:, g:g + 1],
                    in_=bg_d.rearrange("(g p) -> g p", p=P)[g].unsqueeze(1),
                ).then_inc(s_prm, 16)
            sync.dma_start(
                out=SCR[:, 0:D],
                in_=gam_d.unsqueeze(0).broadcast_to([P, D]),
            ).then_inc(s_prm, 16)
            sync.dma_start(
                out=SCR[:, D:2 * D],
                in_=bet_d.unsqueeze(0).broadcast_to([P, D]),
            ).then_inc(s_prm, 16)
            sync.dma_start(out=wrstg, in_=Wr_d).then_inc(s_prm, 16)
            sync.dma_start(out=idb[:, :], in_=ident_bf_d.ap()).then_inc(s_prm, 16)
            sync.dma_start(out=idf[:, :], in_=ident_f_d.ap()).then_inc(s_prm, 16)
            sync.dma_start(
                out=v2(XT[1], "p (t f) -> p t f", t=TPC),
                in_=x_d[CH:2 * CH, :].rearrange("(t p) f -> p t f", p=P),
            ).then_inc(s_x[1], 16)

            sync.wait_ge(s_ini, 1)
            for j in range(NJ + 1):
                if j >= 9 and (j - 1) % 2 == 0:
                    jj = j - 1
                    sync.wait_ge(s_cst, jj + 2)
                    e, hc = divmod(jj - 8, HC)
                    sync.dma_start(
                        out=ews_d[e, hc * P:(hc + 1) * P, :],
                        in_=stgb[jj % 2],
                    ).then_inc(s_scst[jj % 2], 16)
                if j < NJ and j % 2 == 0:
                    if j >= 2:
                        sync.wait_ge(s_cst, (j - 2) + (1 if j - 2 < 8 else 2))
                    if j < 8:
                        sync.dma_start(
                            out=stg[j % 2], in_=Wg_d[j * P:(j + 1) * P, :]
                        ).then_inc(s_stg[j % 2], 16)
                    else:
                        e, hc = divmod(j - 8, HC)
                        sync.dma_start(
                            out=stg[j % 2], in_=EW_d[e, hc * P:(hc + 1) * P, :]
                        ).then_inc(s_stg[j % 2], 16)

            for c in range(NCH):
                if c >= 1:
                    for t in range(TPC):
                        sync.wait_ge(s_fin, (c - 1) * TPC + t + 1)
                        sync.dma_start(
                            out=out_d[(c - 1) * CH + t * P:
                                      (c - 1) * CH + (t + 1) * P, :],
                            in_=XT[(c - 1) % 2][:, t * D:(t + 1) * D],
                        ).then_inc(s_out, 16)
                    if c + 1 < NCH:
                        sync.wait_ge(s_out, 16 * TPC * c)
                        sync.dma_start(
                            out=v2(XT[(c + 1) % 2], "p (t f) -> p t f", t=TPC),
                            in_=x_d[(c + 1) * CH:(c + 2) * CH, :].rearrange(
                                "(t p) f -> p t f", p=P),
                        ).then_inc(s_x[(c + 1) % 2], 16)
                for e in range(E):
                    k = c * E + e
                    if c == 0 and e == 0:
                        sync.wait_ge(s_scst[0], 16 * HC * E // 2)
                        sync.wait_ge(s_scst[1], 16 * HC * E // 2)
                    for half in range(2):
                        q = 2 * k + half
                        if k >= 2:
                            sync.wait_ge(s_mm, 4 * (k - 1))
                        sync.dma_start(
                            out=Wbuf[q % 4][:, :].rearrange(
                                "p (hc o) -> p hc o", hc=HC),
                            in_=ews_d[e].rearrange("(hc p) o -> p hc o", p=P)[
                                :, :, half * 512:(half + 1) * 512],
                        ).then_inc(s_w[q % 4], 16)
            c = NCH
            for t in range(TPC):
                sync.wait_ge(s_fin, (c - 1) * TPC + t + 1)
                sync.dma_start(
                    out=out_d[(c - 1) * CH + t * P:(c - 1) * CH + (t + 1) * P, :],
                    in_=XT[(c - 1) % 2][:, t * D:(t + 1) * D],
                ).then_inc(s_out, 16)
            sync.wait_ge(s_out, 16 * TPC * NCH)

        @block.tensor
        def _(tensor):
            tensor.wait_ge(s_prm, 16 * N_PRM)
            for ec in range(8):
                if ec > 0:
                    tensor.wait_ge(s_gbm, ec)
                tensor.wait_ge(s_cst, ec + 1)
                for hc in range(HC):
                    ins = tensor.transpose(
                        pb_tb[:, hc * P:(hc + 1) * P],
                        in_=stgb[ec % 2][:, hc * P:(hc + 1) * P],
                        identity=idb[:, :],
                    )
                ins.then_inc(s_wgt, 1)
            tensor.wait_ge(s_gbm, 8)
            tensor.wait_ge(s_cst, 9)
            for hc in range(HC):
                ins = tensor.transpose(
                    pb_tb[:, hc * 8:hc * 8 + 8],
                    in_=wrbf[0:8, hc * P:(hc + 1) * P],
                    identity=idb[0:8, 0:8],
                )
            ins.then_inc(s_wgt, 1)

            for c in range(NCH):
                for t in range(TPC):
                    tensor.wait_ge(s_ln, c * TPC + t + 1)
                    tensor.wait_ge(s_tpe, c * TPC + t)
                    for hc in range(HC):
                        ins = tensor.transpose(
                            pb_tb[:, hc * P:(hc + 1) * P],
                            in_=XN[:, t * D + hc * P:t * D + (hc + 1) * P],
                            identity=idb[:, :],
                        )
                    ins.then_inc(s_tpp, 1)
                tensor.wait_ge(s_tpe, c * TPC + TPC)
                if c == 0:
                    tensor.wait_ge(s_gbm, 8)
                for ec in range(8):
                    if ec == 0:
                        if c > 0:
                            tensor.wait_ge(s_rte, c)
                    elif ec == 1:
                        if c > 0:
                            tensor.wait_ge(s_gexp, 8 * c)
                    else:
                        tensor.wait_ge(s_gexp, c * 8 + ec - 1)
                    for hc in range(HC):
                        ins = tensor.matmul(
                            pb_g[ec % 2][:, :],
                            lhsT=WgT[:, hc * D + ec * P:hc * D + (ec + 1) * P],
                            rhs=XNF[:, hc * CH:(hc + 1) * CH],
                            start=(hc == 0), stop=(hc == HC - 1),
                        )
                    ins.then_inc(s_gate, 1)
                if c == 0:
                    tensor.wait_ge(s_ini, 2)
                tensor.wait_ge(s_gexp, c * 8 + 7)
                for hc in range(HC):
                    ins = tensor.matmul(
                        pb_g[0][0:8, :],
                        lhsT=WrT[:, hc * E:(hc + 1) * E],
                        rhs=XNF[:, hc * CH:(hc + 1) * CH],
                        start=(hc == 0), stop=(hc == HC - 1),
                    )
                ins.then_inc(s_rt, 1)
                tensor.wait_ge(s_rte, c + 1)
                tensor.wait_ge(s_yte, 8 * c)
                for t in range(TPC):
                    ins = tensor.transpose(
                        pb_t[:, t * 8:(t + 1) * 8],
                        in_=RFM[0:8, t * P:(t + 1) * P],
                        identity=idf[0:8, 0:8],
                    )
                ins.then_inc(s_rtt, 1)
                for e in range(E):
                    k = c * E + e
                    for half in range(2):
                        q = 2 * k + half
                        tensor.wait_ge(s_w[q % 4], 16 * (q // 4 + 1))
                    for t in range(TPC):
                        et = 4 * k + t
                        if et >= 2:
                            tensor.wait_ge(s_cev, et - 1)
                        for hc in range(HC):
                            lhsT = XNF[:, hc * CH + t * P:hc * CH + (t + 1) * P]
                            tensor.matmul(
                                pb_e[2 * (t % 2)][:, :], lhsT=lhsT,
                                rhs=Wbuf[(2 * k) % 4][:, hc * 512:(hc + 1) * 512],
                                start=(hc == 0), stop=(hc == HC - 1),
                            )
                            ins = tensor.matmul(
                                pb_e[2 * (t % 2) + 1][:, :], lhsT=lhsT,
                                rhs=Wbuf[(2 * k + 1) % 4][:, hc * 512:(hc + 1) * 512],
                                start=(hc == 0), stop=(hc == HC - 1),
                            )
                        ins.then_inc(s_mm, 1)
                tensor.wait_ge(s_rm, (c + 1) * TPC)
                for t in range(TPC):
                    for r in range(2):
                        tensor.wait_ge(s_yte, c * 8 + t * 2 + r)
                        for gi in range(4):
                            g = r * 4 + gi
                            tensor.wait_ge(s_y2, c * 8 + g + 1)
                            ins = tensor.transpose(
                                pb_t[:, gi * P:(gi + 1) * P],
                                in_=Y[:, g * CH + t * P:g * CH + (t + 1) * P],
                                identity=idf[:, :],
                            )
                        ins.then_inc(s_ytp, 1)

        @block.scalar
        def _(scalar):
            for c in range(NCH):
                scalar.wait_ge(s_x[c % 2], 16 * (c // 2 + 1))
                if c > 0:
                    scalar.wait_ge(s_tpp, 4 * c)
                for t in range(TPC):
                    scalar.activation(
                        out=sqd[:, :], in_=XT[c % 2][:, t * D:(t + 1) * D],
                        func=AF.Square, accum_out=LNS[:, t * 8 + 1:t * 8 + 2],
                    ).then_inc(s_sq, 1)
                scalar.wait_ge(s_lnv, c * TPC + TPC)
                for t in range(TPC):
                    o = t * 8
                    scalar.activation(
                        out=LNS[:, o + 4:o + 5], in_=LNS[:, o + 3:o + 4],
                        func=AF.Ln, bias=epsc[:, :], scale=-1.0)
                    scalar.activation(
                        out=LNS[:, o + 5:o + 6], in_=LNS[:, o + 4:o + 5],
                        func=AF.Exp, scale=-0.5,
                    ).then_inc(s_lnr, 1)
                scalar.wait_ge(s_lns, c * TPC + TPC)
                for t in range(TPC):
                    o = t * 8
                    scalar.activation(
                        out=XN[:, t * D:(t + 1) * D],
                        in_=XT[c % 2][:, t * D:(t + 1) * D],
                        func=AF.Identity,
                        scale=LNS[:, o + 5:o + 6], bias=LNS[:, o + 6:o + 7],
                    ).then_inc(s_ln, 1)
                for ec in range(8):
                    scalar.wait_ge(s_gate, c * 8 + ec + 1)
                    if c * 8 + ec >= 1:
                        scalar.wait_ge(s_xg, c * 8 + ec)
                    scalar.activation(
                        out=GE1[:, :],
                        in_=pb_g[ec % 2][:, :],
                        func=AF.Sigmoid, scale=1.0, bias=nbg[:, ec:ec + 1],
                    ).then_inc(s_gexp, 1)
                for t in range(TPC):
                    o = t * 8
                    scalar.wait_ge(s_rx1, c * TPC + t + 1)
                    scalar.activation(
                        out=RE[:, t * E:(t + 1) * E], in_=RT[:, t * E:(t + 1) * E],
                        func=AF.Exp, bias=Rsc[:, o:o + 1],
                    ).then_inc(s_rx2, 1)
                for e in range(E):
                    k = c * E + e
                    for t in range(TPC):
                        et = 4 * k + t
                        scalar.wait_ge(s_mm, et + 1)
                        scalar.wait_ge(s_rm, c * TPC + t + 1)
                        if et >= 2:
                            scalar.wait_ge(s_cmb, et - 1)
                        tb = tmpbf if et % 2 == 0 else tmpbf2
                        mcol = Rm[:, t * E + e:t * E + e + 1]
                        scalar.activation(out=tb[0][:, :],
                                          in_=pb_e[2 * (t % 2)][:, :],
                                          func=AF.Identity, scale=mcol)
                        scalar.activation(out=tb[1][:, :],
                                          in_=pb_e[2 * (t % 2) + 1][:, :],
                                          func=AF.Identity, scale=mcol
                                          ).then_inc(s_cev, 1)
                for t in range(TPC):
                    o = t * 8
                    scalar.wait_ge(s_yte, c * 8 + t * 2 + 2)
                    scalar.activation(
                        out=sqd[:, :], in_=LNT,
                        func=AF.Square, accum_out=LNS2[:, o + 1:o + 2],
                    ).then_inc(s2_sq, 1)
                    scalar.wait_ge(s2_v, c * TPC + t + 1)
                    scalar.activation(
                        out=LNS2[:, o + 4:o + 5], in_=LNS2[:, o + 3:o + 4],
                        func=AF.Ln, bias=epsc[:, :], scale=-1.0)
                    scalar.activation(
                        out=LNS2[:, o + 5:o + 6], in_=LNS2[:, o + 4:o + 5],
                        func=AF.Exp, scale=-0.5,
                    ).then_inc(s2_r, 1)
                    scalar.wait_ge(s2_s, c * TPC + t + 1)
                    scalar.activation(
                        out=LNT, in_=LNT, func=AF.Identity,
                        scale=LNS2[:, o + 5:o + 6], bias=LNS2[:, o + 6:o + 7],
                    ).then_inc(s2_ap, 1)

        @block.vector
        def _(vector):
            vector.wait_ge(s_prm, 16 * N_PRM)
            vector.memset(epsc[:, :], EPS)
            vector.memset(padt[:, :], 0.0)
            vector.tensor_copy(out=gbc[:, :], in_=SCR[:, 0:D])
            vector.tensor_copy(out=bbc[:, :], in_=SCR[:, D:2 * D]
                               ).then_inc(s_ini, 1)
            SCRb = SCR[:, :].bitcast(BF16)
            Yb = Y[:, :].bitcast(BF16)

            def tree_fold(base_len, q0, final_out, outer, buf=None):
                b = SCRb if buf is None else buf
                q = q0
                while q > 2:
                    h = q // 2
                    vw = b[:, 0:base_len].rearrange(
                        "p (o q) -> p o q", o=outer)
                    vector.tensor_add(out=vw[:, :, 0:h], in0=vw[:, :, 0:h],
                                      in1=vw[:, :, h:q])
                    if outer * h < 768:
                        vector.tensor_copy(out=padt[:, 0:384], in_=padt[:, 0:384])
                    q = h
                vw = b[:, 0:base_len].rearrange("p (o q) -> p o q", o=outer)
                vector.tensor_add(out=final_out, in0=vw[:, :, 0],
                                  in1=vw[:, :, 1])

            def tree_fold_gen(base_len, q0, final_out, outer, buf):
                q = q0
                while q > 2:
                    h = q // 2
                    vw = buf[:, 0:base_len].rearrange(
                        "p (o q) -> p o q", o=outer)
                    vector.tensor_add(out=vw[:, :, 0:h], in0=vw[:, :, 0:h],
                                      in1=vw[:, :, h:q]); yield
                    if outer * h < 768:
                        vector.tensor_copy(out=padt[:, 0:384],
                                           in_=padt[:, 0:384]); yield
                    q = h
                vw = buf[:, 0:base_len].rearrange("p (o q) -> p o q", o=outer)
                vector.tensor_add(out=final_out, in0=vw[:, :, 0],
                                  in1=vw[:, :, 1]); yield

            def table_gen():
                rAf = v2(Afm, "p (g n m) -> p g n m", g=NG, n=N)
                rBp = v2(Bpf, "p (g n) -> p g n", g=NG)
                cprev, cnext = cu0, cu1
                vector.tensor_copy(out=cprev[:, :], in_=Cpf[:, :]); yield
                rkt = v2(ktab, "p (g j) -> p g j", g=NG)
                vector.tensor_mul(
                    out=Y[:, 0:NG * N].rearrange("p (g n) -> p g n", g=NG),
                    in0=rBp, in1=v2(cprev, "p (g n) -> p g n", g=NG)); yield
                vector.reduce_sum(
                    out=rkt[:, :, 0], in_=Y[:, 0:NG * N].rearrange(
                        "p (g n) -> p g n", g=NG), axis=AX.X); yield
                rC2 = v2(C2, "p (g w n) -> p g w n", g=NG, w=W)
                for j in range(1, W + 1):
                    vector.tensor_mul(
                        out=Y[:, 0:NG * N * N].rearrange(
                            "p (g n m) -> p g n m", g=NG, n=N),
                        in0=v2(cprev, "p (g m) -> p g m", g=NG).unsqueeze(2)
                            .broadcast_to([P, NG, N, N]),
                        in1=rAf); yield
                    vector.reduce_sum(
                        out=v2(cnext, "p (g n) -> p g n", g=NG),
                        in_=Y[:, 0:NG * N * N].rearrange(
                            "p (g n m) -> p g n m", g=NG, n=N),
                        axis=AX.X); yield
                    vector.tensor_copy(
                        out=rC2[:, :, j - 1, :],
                        in_=v2(cnext, "p (g n) -> p g n", g=NG)); yield
                    if j <= W - 1:
                        vector.tensor_mul(
                            out=Y[:, 0:NG * N].rearrange(
                                "p (g n) -> p g n", g=NG),
                            in0=rBp, in1=v2(cnext, "p (g n) -> p g n", g=NG)); yield
                        vector.reduce_sum(
                            out=rkt[:, :, j], in_=Y[:, 0:NG * N].rearrange(
                                "p (g n) -> p g n", g=NG), axis=AX.X); yield
                    cprev, cnext = cnext, cprev
                vector.scalar_tensor_tensor(
                    out=rkt[:, :, 0:1], in0=v2(Dpf, "p g -> p g").unsqueeze(2),
                    scalar=1.0, in1=rkt[:, :, 0:1],
                    op0=ALU.add, op1=ALU.add); yield
                Ab = Yb[:, 2 * NG * N * N:2 * NG * N * N + 2048]
                vector.tensor_copy(out=Ab, in_=Afm[:, :]); yield
                Mcur = Yb[:, 0:NG * N * N]
                Mnxt = Yb[:, NG * N * N:2 * NG * N * N]
                vector.tensor_copy(out=Mcur, in_=Ab); yield
                for sq_i in range(NSQ):
                    last = sq_i == NSQ - 1
                    for g in range(NG):
                        vector.tensor_mul(
                            out=U2[:, 0:N * N * N].rearrange(
                                "p (n m q) -> p n m q", n=N, m=N),
                            in0=Mcur.rearrange("p (g n q) -> p g n q",
                                               g=NG, n=N)[:, g]
                                .unsqueeze(2).broadcast_to([P, N, N, N]),
                            in1=Mcur.rearrange("p (g n m) -> p g n m",
                                               g=NG, n=N)[:, g]
                                .transpose([0, 2, 1]).unsqueeze(1)
                                .broadcast_to([P, N, N, N])); yield
                        if not last:
                            yield from tree_fold_gen(
                                N * N * N, N,
                                Mnxt.rearrange("p (g n m) -> p g n m",
                                               g=NG, n=N)[:, g], N * N, U2)
                        else:
                            yield from tree_fold_gen(
                                N * N * N, N,
                                v2(M2T, "p (g m n) -> p g m n",
                                   g=NG, m=N)[:, g]
                                .transpose([0, 2, 1]), N * N, U2)
                    if not last:
                        Mcur, Mnxt = Mnxt, Mcur
                rU2 = v2(U2, "p (g n w) -> p g n w", g=NG, n=N)
                pprev, pnext = cu0, cu1
                vector.tensor_copy(out=pprev[:, :], in_=Bpf[:, :]); yield
                vector.tensor_copy(
                    out=rU2[:, :, :, W - 1],
                    in_=v2(pprev, "p (g n) -> p g n", g=NG)); yield
                for j in range(1, W):
                    vector.tensor_mul(
                        out=Y[:, 0:NG * N * N].rearrange(
                            "p (g m n) -> p g m n", g=NG, m=N),
                        in0=v2(pprev, "p (g n) -> p g n", g=NG).unsqueeze(2)
                            .broadcast_to([P, NG, N, N]),
                        in1=rAf.transpose([0, 1, 3, 2])); yield
                    vector.reduce_sum(
                        out=v2(pnext, "p (g m) -> p g m", g=NG),
                        in_=Y[:, 0:NG * N * N].rearrange(
                            "p (g m n) -> p g m n", g=NG, m=N),
                        axis=AX.X); yield
                    vector.tensor_copy(
                        out=rU2[:, :, :, W - 1 - j],
                        in_=v2(pnext, "p (g n) -> p g n", g=NG)); yield
                    pprev, pnext = pnext, pprev

            tg = table_gen()

            _SENT = object()

            def pump(n):
                for _ in range(n):
                    if next(tg, _SENT) is _SENT:
                        return

            for ec in range(8):
                pump(2)
                vector.wait_ge(s_stg[ec % 2], 16 * (ec // 2 + 1))
                if ec >= 2:
                    vector.wait_ge(s_wgt, ec - 1)
                vector.tensor_copy(out=stgb[ec % 2], in_=stg[ec % 2]
                                   ).then_inc(s_cst, 1)
                vector.wait_ge(s_wgt, ec + 1)
                vector.tensor_copy(
                    out=v2(WgT, "p (hc f) -> p hc f", hc=HC)[:, :, ec * P:(ec + 1) * P],
                    in_=v2(pb_tb, "p (hc q) -> p hc q", hc=HC),
                ).then_inc(s_gbm, 1)
            vector.tensor_copy(out=wrbf[0:E, :], in_=wrstg).then_inc(s_cst, 1)
            vector.wait_ge(s_wgt, 9)
            vector.tensor_copy(out=WrT[:, :], in_=pb_tb[:, 0:HC * 8])
            vector.tensor_copy(out=padt[:, 0:1024], in_=padt[:, 0:1024]
                               ).then_inc(s_ini, 1)
            for j in range(8, NJ):
                pump(5)
                vector.wait_ge(s_stg[j % 2], 16 * (j // 2 + 1))
                if j < 10:
                    vector.wait_ge(s_wgt, j - 1)
                else:
                    vector.wait_ge(s_scst[j % 2], 16 * ((j - 10) // 2 + 1))
                vector.tensor_copy(out=stgb[j % 2], in_=stg[j % 2]
                                   ).then_inc(s_cst, 1)

            pump(10 ** 9)
            vector.memset(Scar[:, :], 0.0)
            rS = v2(S, "p (g i n) -> p g i n", g=NG, i=ISUB)
            rG = v2(G, "p (g i n) -> p g i n", g=NG, i=ISUB)
            rM2 = v2(M2T, "p (g m n) -> p g m n", g=NG, m=N)
            rU2b = v2(U2, "p (g n w) -> p g n w", g=NG, n=N)
            rC2b = v2(C2, "p (g w n) -> p g w n", g=NG, w=W)

            for c in range(NCH):
                vector.wait_ge(s_x[c % 2], 16 * (c // 2 + 1))
                for t in range(TPC):
                    vector.reduce_sum(out=LNS[:, t * 8:t * 8 + 1],
                                      in_=XT[c % 2][:, t * D:(t + 1) * D],
                                      axis=AX.X)
                vector.wait_ge(s_sq, c * TPC + TPC)
                for t in range(TPC):
                    o = t * 8
                    vector.tensor_scalar(out=LNS[:, o + 2:o + 3],
                                         in0=LNS[:, o:o + 1],
                                         scalar1=1.0 / D, scalar2=None,
                                         op0=ALU.mult)
                for t in range(TPC):
                    o = t * 8
                    vector.tensor_scalar(out=LNS[:, o + 3:o + 4],
                                         in0=LNS[:, o + 1:o + 2],
                                         scalar1=1.0 / D, scalar2=None,
                                         op0=ALU.mult)
                for t in range(TPC):
                    o = t * 8
                    vector.scalar_tensor_tensor(
                        out=LNS[:, o + 3:o + 4], in0=LNS[:, o + 2:o + 3],
                        scalar=LNS[:, o + 2:o + 3], in1=LNS[:, o + 3:o + 4],
                        op0=ALU.mult, op1=ALU.subtract).then_inc(s_lnv, 1)
                vector.wait_ge(s_lnr, c * TPC + TPC)
                for t in range(TPC):
                    o = t * 8
                    vector.tensor_scalar(out=LNS[:, o + 6:o + 7],
                                         in0=LNS[:, o + 2:o + 3],
                                         scalar1=LNS[:, o + 5:o + 6],
                                         scalar2=-1.0,
                                         op0=ALU.mult, op1=ALU.mult
                                         ).then_inc(s_lns, 1)
                for t in range(TPC):
                    vector.wait_ge(s_tpp, c * TPC + t + 1)
                    if c > 0 and t == 0:
                        vector.wait_ge(s_mm, 32 * c)
                    vector.tensor_copy(
                        out=v2(XNF, "p (hc ch) -> p hc ch", hc=HC)[
                            :, :, t * P:(t + 1) * P],
                        in_=v2(pb_tb, "p (hc q) -> p hc q", hc=HC),
                    ).then_inc(s_tpe, 1)
                for ec in range(8):
                    vector.wait_ge(s_gexp, c * 8 + ec + 1)
                    vector.tensor_mul(
                        out=XGF[:, ec * CH:(ec + 1) * CH],
                        in0=XNF[:, ec * CH:(ec + 1) * CH], in1=GE1[:, :],
                    ).then_inc(s_xg, 1)
                vector.wait_ge(s_rt, c + 1)
                vector.tensor_copy(out=RFM[:, :], in_=pb_g[0][0:8, :]
                                   ).then_inc(s_rte, 1)
                vector.wait_ge(s_rtt, c + 1)
                for t in range(TPC):
                    vector.reduce_max(out=Rsc[:, t * 8:t * 8 + 1],
                                      in_=pb_t[:, t * 8:(t + 1) * 8], axis=AX.X,
                                      negate=True).then_inc(s_rx1, 1)
                vector.tensor_copy(out=RT[:, :], in_=pb_t[:, 0:TPC * 8])
                LG = lambda t: RT[:, t * E:(t + 1) * E]
                EXs = lambda t: RE[:, t * E:(t + 1) * E]
                SEL = lambda t: RSel[:, t * E:(t + 1) * E]
                MM = lambda t: Rm[:, t * E:(t + 1) * E]
                RS = lambda t, i: Rsc[:, t * 8 + i:t * 8 + i + 1]
                vector.wait_ge(s_rx2, c * TPC + TPC)
                for t in range(TPC):
                    vector.tensor_scalar(out=SEL(t), in0=LG(t), scalar1=RS(t, 0),
                                         scalar2=None, op0=ALU.add)
                for t in range(TPC):
                    vector.tensor_scalar(out=SEL(t), in0=SEL(t), scalar1=0.0,
                                         scalar2=None, op0=ALU.is_ge)
                for t in range(TPC):
                    vector.scalar_tensor_tensor(out=MM(t), in0=SEL(t),
                                                scalar=-BIG, in1=LG(t),
                                                op0=ALU.mult, op1=ALU.add)
                for t in range(TPC):
                    vector.reduce_max(out=RS(t, 2), in_=MM(t), axis=AX.X,
                                      negate=True)
                for t in range(TPC):
                    vector.tensor_scalar(out=SEL(t), in0=LG(t), scalar1=RS(t, 2),
                                         scalar2=None, op0=ALU.add)
                for t in range(TPC):
                    vector.tensor_scalar(out=SEL(t), in0=SEL(t), scalar1=0.0,
                                         scalar2=None, op0=ALU.is_ge)
                for t in range(TPC):
                    vector.tensor_mul(out=SEL(t), in0=SEL(t), in1=EXs(t))
                for t in range(TPC):
                    vector.reduce_sum(out=RS(t, 4), in_=SEL(t), axis=AX.X)
                for t in range(TPC):
                    vector.reduce_sum(out=RS(t, 1), in_=EXs(t), axis=AX.X)
                for t in range(TPC):
                    vector.reciprocal(out=RS(t, 3), in_=RS(t, 1))
                for t in range(TPC):
                    vector.tensor_mul(out=RS(t, 4), in0=RS(t, 4), in1=RS(t, 3))
                for t in range(TPC):
                    vector.tensor_scalar(out=RS(t, 4), in0=RS(t, 4),
                                         scalar1=1e-6, scalar2=None, op0=ALU.add)
                for t in range(TPC):
                    vector.reciprocal(out=RS(t, 4), in_=RS(t, 4))
                for t in range(TPC):
                    vector.tensor_mul(out=RS(t, 5), in0=RS(t, 3), in1=RS(t, 4))
                for t in range(TPC):
                    vector.tensor_scalar(out=MM(t), in0=SEL(t), scalar1=RS(t, 5),
                                         scalar2=None, op0=ALU.mult)
                for t in range(TPC):
                    vector.tensor_mul(out=EXs(t), in0=MM(t), in1=SEL(t))
                for t in range(TPC):
                    vector.reduce_sum(out=RS(t, 6), in_=EXs(t), axis=AX.X)
                for t in range(TPC):
                    vector.tensor_mul(out=RS(t, 7), in0=RS(t, 6), in1=RS(t, 3)
                                      ).then_inc(s_rm, 1)
                for t in range(TPC):
                    vector.tensor_scalar(out=FF[:, t * D:(t + 1) * D],
                                         in0=XT[c % 2][:, t * D:(t + 1) * D],
                                         scalar1=0.0, scalar2=RS(t, 7),
                                         op0=ALU.mult, op1=ALU.add)
                def scan_gen(c=c):
                    vector.wait_ge(s_ytp, 8 * c)
                    vector.tensor_copy(out=rS[:, :, 0, :],
                                       in_=v2(Scar, "p (g n) -> p g n", g=NG))
                    vector.tensor_copy(out=padt[:, 0:512], in_=padt[:, 0:512]
                                       ).then_inc(s_gc, 1); yield
                    for g in range(NG):
                        xg_g = XGF[:, g * CH:(g + 1) * CH]
                        vector.tensor_mul(
                            out=SCRb.rearrange("p (i n w) -> p i n w",
                                               i=ISUB, n=N),
                            in0=xg_g.rearrange("p (i w) -> p i w", i=ISUB)
                                .unsqueeze(2).broadcast_to([P, ISUB, N, W]),
                            in1=rU2b[:, g].unsqueeze(1)
                                .broadcast_to([P, ISUB, N, W])); yield
                        yield from tree_fold_gen(ISUB * N * W, W, rG[:, g],
                                                 ISUB * N, SCRb)
                    for i in range(ISUB):
                        vector.tensor_mul(
                            out=SCRb[:, 0:NG * N * N].rearrange(
                                "p (g m n) -> p g m n", g=NG, m=N),
                            in0=rS[:, :, i, :].unsqueeze(2)
                                .broadcast_to([P, NG, N, N]),
                            in1=rM2); yield
                        yield from tree_fold_gen(
                            NG * N * N, N,
                            v2(cub, "p (g m) -> p g m", g=NG), NG * N, SCRb)
                        vector.tensor_copy(out=padt[:, 0:384],
                                           in_=padt[:, 0:384]); yield
                        tgt = rS[:, :, i + 1, :] if i + 1 < ISUB else \
                            v2(Scar, "p (g n) -> p g n", g=NG)
                        vector.tensor_add(out=tgt,
                                          in0=v2(cub, "p (g m) -> p g m", g=NG),
                                          in1=rG[:, :, i, :]); yield
                        vector.tensor_copy(out=padt[:, 0:512],
                                           in_=padt[:, 0:512]); yield
                    xg_g = lambda g: XGF[:, g * CH:(g + 1) * CH]
                    y_g = lambda g: Y[:, g * CH:(g + 1) * CH]
                    for g in range(NG):
                        vector.tensor_scalar(
                            out=y_g(g), in0=xg_g(g),
                            scalar1=ktab[:, g * W:g * W + 1], scalar2=None,
                            op0=ALU.mult); yield
                    for j in range(1, W):
                        for g in range(NG):
                            vector.scalar_tensor_tensor(
                                out=y_g(g).rearrange("p (i w) -> p i w",
                                                     i=ISUB)[:, :, j:W],
                                in0=xg_g(g).rearrange("p (i w) -> p i w",
                                                      i=ISUB)[:, :, 0:W - j],
                                scalar=ktab[:, g * W + j:g * W + j + 1],
                                in1=y_g(g).rearrange("p (i w) -> p i w",
                                                     i=ISUB)[:, :, j:W],
                                op0=ALU.mult, op1=ALU.add); yield
                    for g in range(NG):
                        vector.tensor_mul(
                            out=SCRb.rearrange("p (i w n) -> p i w n",
                                               i=ISUB, w=W),
                            in0=rS[:, g].unsqueeze(2)
                                .broadcast_to([P, ISUB, W, N]),
                            in1=rC2b[:, g].unsqueeze(1)
                                .broadcast_to([P, ISUB, W, N])); yield
                        yield from tree_fold_gen(
                            ISUB * W * N, N,
                            v2(Yst, "p (i w) -> p i w", i=ISUB), ISUB * W, SCRb)
                        vector.tensor_add(out=y_g(g), in0=y_g(g), in1=Yst[:, :]
                                          ).then_inc(s_y2, 1); yield

                sg = scan_gen()

                def spump(n):
                    for _ in range(n):
                        if next(sg, _SENT) is _SENT:
                            return

                for e in range(E):
                    k = c * E + e
                    for t in range(TPC):
                        et = 4 * k + t
                        vector.wait_ge(s_cev, et + 1)
                        tb = tmpbf if et % 2 == 0 else tmpbf2
                        for half in range(2):
                            sl = slice(t * D + half * 512, t * D + (half + 1) * 512)
                            ins = vector.tensor_add(out=FF[:, sl], in0=FF[:, sl],
                                                    in1=tb[half][:, :])
                        ins.then_inc(s_cmb, 1)
                        spump(8)
                spump(10 ** 9)
                for t in range(TPC):
                    o = t * 8
                    for r in range(2):
                        vector.wait_ge(s_ytp, c * 8 + t * 2 + r + 1)
                        vector.tensor_copy(
                            out=Afm[:, r * 512:(r + 1) * 512],
                            in_=pb_t[:, :]).then_inc(s_yte, 1)
                    vector.reduce_sum(out=LNS2[:, o:o + 1], in_=LNT,
                                      axis=AX.X)
                    vector.wait_ge(s2_sq, c * TPC + t + 1)
                    vector.tensor_scalar(out=LNS2[:, o + 2:o + 3],
                                         in0=LNS2[:, o:o + 1],
                                         scalar1=1.0 / D, scalar2=None,
                                         op0=ALU.mult)
                    vector.tensor_scalar(out=LNS2[:, o + 3:o + 4],
                                         in0=LNS2[:, o + 1:o + 2],
                                         scalar1=1.0 / D, scalar2=None,
                                         op0=ALU.mult)
                    vector.tensor_copy(out=padt[:, 0:512], in_=padt[:, 0:512])
                    vector.scalar_tensor_tensor(
                        out=LNS2[:, o + 3:o + 4], in0=LNS2[:, o + 2:o + 3],
                        scalar=LNS2[:, o + 2:o + 3], in1=LNS2[:, o + 3:o + 4],
                        op0=ALU.mult, op1=ALU.subtract).then_inc(s2_v, 1)
                    vector.wait_ge(s2_r, c * TPC + t + 1)
                    vector.tensor_scalar(out=LNS2[:, o + 6:o + 7],
                                         in0=LNS2[:, o + 2:o + 3],
                                         scalar1=LNS2[:, o + 5:o + 6],
                                         scalar2=-1.0,
                                         op0=ALU.mult, op1=ALU.mult
                                         ).then_inc(s2_s, 1)
                    vector.wait_ge(s2_ap, c * TPC + t + 1)
                    vector.tensor_mul(out=LNT, in0=LNT, in1=gbc[:, :])
                    vector.tensor_add(out=LNT, in0=LNT, in1=bbc[:, :])
                    xt_t = XT[c % 2][:, t * D:(t + 1) * D]
                    vector.tensor_add(out=xt_t, in0=xt_t, in1=LNT)
                    vector.tensor_add(out=xt_t, in0=xt_t,
                                      in1=FF[:, t * D:(t + 1) * D]
                                      ).then_inc(s_fin, 1)

        @block.gpsimd
        def _(gpsimd):
            gpsimd.wait_ge(s_ini, 1)
            for j in range(NJ + 1):
                if j >= 9 and (j - 1) % 2 == 1:
                    jj = j - 1
                    gpsimd.wait_ge(s_cst, jj + 2)
                    e, hc = divmod(jj - 8, HC)
                    gpsimd.dma_start(
                        out=ews_d[e, hc * P:(hc + 1) * P, :],
                        in_=stgb[jj % 2],
                    ).then_inc(s_scst[jj % 2], 16)
                if j < NJ and j % 2 == 1:
                    if j >= 2:
                        gpsimd.wait_ge(s_cst, (j - 2) + (1 if j - 2 < 8 else 2))
                    if j < 8:
                        gpsimd.dma_start(
                            out=stg[j % 2], in_=Wg_d[j * P:(j + 1) * P, :]
                        ).then_inc(s_stg[j % 2], 16)
                    else:
                        e, hc = divmod(j - 8, HC)
                        gpsimd.dma_start(
                            out=stg[j % 2], in_=EW_d[e, hc * P:(hc + 1) * P, :]
                        ).then_inc(s_stg[j % 2], 16)
            if DBG:
                nd = 0
                for c in range(NCH):
                    gpsimd.wait_ge(s_xg, c * 8 + 8)
                    gpsimd.dma_start(out=dXG_d[c], in_=XGF[:, :]).then_inc(s_dbg, 16)
                    gpsimd.wait_ge(s_y2, c * 8 + 8)
                    gpsimd.dma_start(out=dY_d[c], in_=Y[:, :]).then_inc(s_dbg, 16)
                    gpsimd.wait_ge(s_cmb, 32 * (c + 1))
                    gpsimd.dma_start(out=dFF_d[c], in_=FF[:, :]).then_inc(s_dbg, 16)
                    gpsimd.wait_ge(s_rm, TPC * (c + 1))
                    gpsimd.dma_start(out=dRm_d[c], in_=Rm[:, :]).then_inc(s_dbg, 16)
                    gpsimd.dma_start(out=dRs_d[c], in_=Rsc[:, :]).then_inc(s_dbg, 16)
                    gpsimd.dma_start(out=dRT_d[c], in_=RT[:, :]).then_inc(s_dbg, 16)
                    gpsimd.dma_start(out=dRE_d[c], in_=RE[:, :]).then_inc(s_dbg, 16)
                    gpsimd.dma_start(out=dSel_d[c], in_=RSel[:, :]).then_inc(s_dbg, 16)
                    gpsimd.wait_ge(s_rte, c + 1)
                    gpsimd.dma_start(out=dXNF_d[c], in_=XNF[:, :]).then_inc(s_dbg, 16)
                    gpsimd.dma_start(out=dRFM_d[c], in_=RFM[:, :]).then_inc(s_dbg, 16)
                    gpsimd.wait_ge(s_ln, c * TPC + TPC)
                    gpsimd.dma_start(out=dXN_d[c], in_=XN[:, :]).then_inc(s_dbg, 16)
                    gpsimd.dma_start(out=dLS_d[c], in_=LNS[:, :]).then_inc(s_dbg, 16)
                    gpsimd.wait_ge(s_gexp, c * 8 + 1)
                    gpsimd.dma_start(out=dGE_d[c], in_=GE1[:, :]).then_inc(s_dbg, 16)
                    gpsimd.dma_start(out=dS_d[c], in_=S[:, :]).then_inc(s_dbg, 16)
                    if c == 0:
                        gpsimd.dma_start(out=dK_d, in_=ktab[:, :]).then_inc(s_dbg, 16)
                        gpsimd.dma_start(out=dU_d, in_=U2[:, :]).then_inc(s_dbg, 16)
                        gpsimd.dma_start(out=dC_d, in_=C2[:, :]).then_inc(s_dbg, 16)
                        nd += 3
                    nd += 14
                    for t in range(TPC):
                        gpsimd.wait_ge(s2_ap, c * TPC + t + 1)
                        gpsimd.dma_start(out=dLN_d[c, t], in_=LNT).then_inc(s_dbg, 16)
                        nd += 1
                gpsimd.wait_ge(s_dbg, 16 * nd)

    return nc


_CACHE = {}


def kernel(**inputs):
    key = "nc"
    if key not in _CACHE:
        _CACHE[key] = build_nc()
    nc = _CACHE[key]

    f32 = lambda a: np.ascontiguousarray(np.asarray(a, dtype=np.float32))
    x = f32(inputs["L0_attn_x"])
    shared = {
        "A": f32(inputs["L0_attn_A"]),
        "Bp": f32(inputs["L0_attn_Bp"]),
        "Cp": f32(inputs["L0_attn_Cp"]),
        "Dp": f32(inputs["L0_attn_Dp"]),
        "Wg": f32(inputs["L0_attn_Wg"]),
        "bg": f32(inputs["L0_attn_bg"]),
        "gamma": f32(inputs["L0_attn_gamma"]),
        "beta": f32(inputs["L0_attn_beta"]),
        "Wr": f32(inputs["L0_ffn_router_w"]),
        "EW": f32(inputs["L0_ffn_expert_weights"]),
    }
    in_maps = [{"x": np.ascontiguousarray(x[b]), **shared} for b in range(N_CORES)]
    res = run_bass_kernel_spmd(nc, in_maps, core_ids=list(range(N_CORES)))
    out = np.stack([res.results[b]["out"] for b in range(N_CORES)], axis=0)
    return out
